# version 03
# baseline (speedup 1.0000x reference)
import contextlib
import numpy as np
import concourse.bass as bass
import concourse.mybir as mybir
from concourse.bass_utils import run_bass_kernel_spmd

F32 = mybir.dt.float32
BF16 = mybir.dt.bfloat16
F16 = mybir.dt.float16
I32 = mybir.dt.int32
ALU = mybir.AluOpType
AF = mybir.ActivationFunctionType
AX = mybir.AxisListType

N, C, H, D, II = 2048, 1024, 16, 64, 2816
NT, NCC, NI = 16, 8, 22
QMAX = 127.0
WPREP = 2 * NI


class Sem:
    def __init__(self, h):
        self.h = h
        self.n = 0

    def inc(self, instr, k=1):
        instr.then_inc(self.h, k)
        self.n += k
        return self.n


class DStream:

    def __init__(self, nc, es, name, depth=1):
        self.sems = [Sem(es.enter_context(nc.semaphore(f"{name}_{d}")))
                     for d in range(depth)]
        self.depth = depth
        self.i = 0
        self.pure = True

    def tok(self, j):
        assert self.pure
        s = self.sems[j % self.depth]
        return (s.h, 16 * (j // self.depth + 1))

    def inc(self, instr):
        s = self.sems[self.i % self.depth]
        instr.then_inc(s.h, 16)
        s.n += 16
        t = (s.h, s.n)
        if self.pure:
            assert t == self.tok(self.i)
        self.i += 1
        return t

    def inc_group(self, instrs):
        s = self.sems[self.i % self.depth]
        for ins in instrs:
            ins.then_inc(s.h, 16)
            s.n += 16
        self.i += 1
        self.pure = False
        return (s.h, s.n)

    def total_toks(self):
        return [(s.h, s.n) for s in self.sems]


def wq(eng, tokv):
    eng.wait_ge(tokv[0], tokv[1])


def wq_all(eng, *streams):
    for st_ in streams:
        for t in st_.total_toks():
            if t[1]:
                eng.wait_ge(t[0], t[1])


def build_nc(taps=(), stop_after=None):
    nc = bass.Bass("TRN2", target_bir_lowering=False, detect_race_conditions=False)
    taps = set(taps)

    _dtsz = {F32: 4, BF16: 2, F16: 2, I32: 4}
    _dbg_live = {}
    build_nc.dbg_live = _dbg_live
    build_nc.dbg_nc = nc
    _regions = {}
    _free_list = []

    def _mgr_init(base, top):
        _free_list.append([base, top])

    def alloc(name, shape, dtype, psum=False):
        assert not psum
        size = shape[1] * _dtsz[dtype]
        size = (size + 63) & ~63
        best = None
        for seg in _free_list:
            room = seg[1] - seg[0]
            if room >= size and (best is None or room < best[1] - best[0]):
                best = seg
        if best is None:
            raise MemoryError(f"sbuf arena full for {name} ({size}B): {_free_list}")
        off = best[0]
        best[0] += size
        _regions[name] = (off, size)
        t = nc.alloc_sbuf_tensor_at(name, shape, dtype, offset=off)
        _dbg_live[name] = (off, size)
        return t

    def free(*names):
        for nm in names:
            off, size = _regions.pop(nm)
            _free_list.append([off, off + size])
        _free_list.sort()
        merged = []
        for seg in _free_list:
            if merged and merged[-1][1] >= seg[0]:
                merged[-1][1] = max(merged[-1][1], seg[1])
            else:
                merged.append(seg)
        _free_list[:] = merged

    x_in = nc.dram_tensor("x", [N, C], F32, kind="ExternalInput").ap()
    wqkv = nc.dram_tensor("w_qkv", [3 * C, C], F32, kind="ExternalInput").ap()
    wproj = nc.dram_tensor("w_proj", [C, C], F32, kind="ExternalInput").ap()
    bproj = nc.dram_tensor("b_proj", [C], F32, kind="ExternalInput").ap()
    wdwc = nc.dram_tensor("w_dwc", [C, 3], F32, kind="ExternalInput").ap()
    bdwc = nc.dram_tensor("b_dwc", [C], F32, kind="ExternalInput").ap()
    clipx = nc.dram_tensor("clip_x", [1], F32, kind="ExternalInput").ap()
    clipd = nc.dram_tensor("clip_down", [1], F32, kind="ExternalInput").ap()
    wgate = nc.dram_tensor("w_gate", [II, C], F32, kind="ExternalInput").ap()
    wup = nc.dram_tensor("w_up", [II, C], F32, kind="ExternalInput").ap()
    wdown = nc.dram_tensor("w_down", [C, II], F32, kind="ExternalInput").ap()
    crow = nc.dram_tensor("c_row", [C], F32, kind="ExternalInput").ap()
    wmod = nc.dram_tensor("wmod", [C, 6 * C], F32, kind="ExternalInput").ap()
    y_out = nc.dram_tensor("y", [N, C], F32, kind="ExternalOutput").ap()

    wqkv_bf = nc.dram_tensor("wqkv_bf", [3 * C, C], BF16).ap()
    wproj_bf = nc.dram_tensor("wproj_bf", [C, C], BF16).ap()
    h_dram = nc.dram_tensor("h_dram", [N, C], BF16).ap()
    xq_dram = nc.dram_tensor("xq_dram", [N, C], BF16).ap()
    wgq_dram = nc.dram_tensor("wgq_dram", [II, C], BF16).ap()
    wuq_dram = nc.dram_tensor("wuq_dram", [II, C], BF16).ap()
    wdq_dram = nc.dram_tensor("wdq_dram", [C, II], BF16).ap()
    h3_dram = nc.dram_tensor("h3_dram", [NI * 128, N], F32).ap()
    x1_dram = nc.dram_tensor("x1_dram", [N, C], F32).ap()
    z_dram = nc.dram_tensor("z_dram", [H, N], BF16).ap()
    row_dram = nc.dram_tensor("row_dram", [8, C], F32).ap()
    vec_dram = nc.dram_tensor("vec_dram", [4, 128], F32).ap()
    cc1_in = nc.dram_tensor("cc1_in", [1, 128], F32)
    cc1_out = nc.dram_tensor("cc1_out", [1, 128], F32)
    cc2_in = nc.dram_tensor("cc2_in", [1, 128], F32)
    cc2_out = nc.dram_tensor("cc2_out", [1, 128], F32)

    tap_outs = {}
    es_sm = contextlib.ExitStack()

    def tapout(name, shape, dtype):
        if name in taps:
            tap_outs[name] = nc.dram_tensor("tap_" + name, shape, dtype, kind="ExternalOutput").ap()
        return tap_outs.get(name)

    tapout("h_fm", [128, NCC * N], BF16)
    tapout("a_fm", [128, NCC * N], BF16)
    tapout("k2", [128, C], BF16)
    tapout("kvst", [128, NCC * 128], BF16)
    tapout("ksmat", [128, NCC * 16], BF16)
    tapout("z", [H, N], BF16)
    tapout("conv", [128, NCC * N], BF16)
    tapout("x1", [N, C], F32)
    tapout("h2", [128, NT * C], F16)
    tapout("s1", [128, 1], F32)
    tapout("xq", [N, C], BF16)
    tapout("h3", [NI * 128, N], F32)
    tapout("s2", [128, 1], F32)
    tapout("mod", [128, 48], F32)
    tapout("wgq", [II, C], BF16)
    tapout("w1t", [128, NCC * 3 * C], BF16)
    tapout("wqkvbf", [3 * C, C], BF16)
    tapout("wprojbf", [C, C], BF16)
    tapout("ca1c", [128, 8], F32)
    tapout("uq", [128, 512], F32)
    tapout("qps", [128, 4096], F32)
    tapout("sq2", [128, 512], F32)
    tapout("w1tl", [128, NCC * 3 * C], BF16)
    tapout("modl", [128, 48], F32)
    tapout("ku", [128, C], F32)
    tapout("swr", [128, H], F32)
    tapout("kps", [128, 2048], F32)
    tapout("hdrm", [N, C], BF16)
    tapout("hfml", [128, NCC * N], BF16)
    tapout("lnst", [128, 64], F32)

    sm = {}
    for nm, sh, dt in [
        ("mod_cols", [128, 48], F32),
        ("ca1c", [128, 8], F32), ("sapc", [128, 8], F32), ("gac", [128, 8], F32),
        ("cb1c", [128, 8], F32), ("gbc", [128, 8], F32), ("sgn", [128, 8], F32),
        ("w0c", [128, 8], F32), ("w1p1c", [128, 8], F32), ("w2c", [128, 8], F32),
        ("bdc", [128, 8], F32), ("bpc", [128, 8], F32), ("gabpc", [128, 8], F32),
        ("ccol", [128, 8], F32), ("siluc", [128, 8], BF16),
        ("st6", [128, 12 * NT], F32), ("mv", [128, 2 * NT], F32),
        ("r1", [128, NT], F32), ("nmr1", [128, NT], F32), ("r1t", [128, NT], F32),
        ("swr", [128, H], F32), ("r2t", [128, 1], F32), ("scr2k", [128, C], F32),
        ("magic", [128, 1], F32),
        ("sgc", [128, NI], F32), ("suc", [128, NI], F32), ("sdc", [128, 8], F32),
        ("wginv", [128, NI], F32), ("wuinv", [128, NI], F32), ("wdinv", [128, 8], F32),
        ("sgatec", [128, NI], F32), ("supc", [128, NI], F32),
        ("gmx", [128, NT], F32), ("h3mx", [128, NI], F32),
        ("onescol", [128, 1], BF16), ("wred", [128, 1], F32), ("dred", [128, 1], F32),
        ("reda", [128, 1], F32), ("redb", [128, 1], F32),
        ("epsln", [128, 1], F32), ("eps6", [128, 1], F32), ("rq", [128, 1], F32),
        ("scal", [128, 16], F32),
        ("tinyrow", [1, 128], F32), ("tiny1", [1, 1], F32),
        ("garep", [128, C], BF16), ("gabp_rep", [128, C], F32),
        ("cb1_rep", [128, C], F32), ("sb_rep", [128, C], F32), ("coef_rep", [128, C], F32),
    ]:
        sm[nm] = es_sm.enter_context(nc.sbuf_tensor(nm, sh, dt))
        _dbg_live["sm_" + nm] = (nc.lookup_mloc(sm[nm]).addr, sh[1] * _dtsz[dt])

    _sentinel = es_sm.enter_context(nc.sbuf_tensor("_sentinel", [128, 16], F32))
    _base = nc.lookup_mloc(_sentinel).addr + 64
    _dbg_live["_sentinel"] = (nc.lookup_mloc(_sentinel).addr, 64)
    _base = (_base + 63) & ~63
    _mgr_init(_base, 229344)
    psall = es_sm.enter_context(nc.psum_tensor("psall", [128, 4096], F32))

    def bank(b, n=1):
        return psall[:, 512 * b: 512 * (b + n)]

    es = contextlib.ExitStack()
    S = {k: Sem(es.enter_context(nc.semaphore(k))) for k in
         ["tv", "vv", "sv", "gv", "ccs", "wpv", "dpg", "va"]}
    tv, vv, sv, gv, ccs, wpv, dpg, va = (
        S[k] for k in ["tv", "vv", "sv", "gv", "ccs", "wpv", "dpg", "va"])

    qsm = DStream(nc, es, "qsm")
    qx = DStream(nc, es, "qx", 4)
    qhs = DStream(nc, es, "qhs", 2)
    qhf = DStream(nc, es, "qhf", 4)
    qw1 = DStream(nc, es, "qw1", 4)
    qwm = DStream(nc, es, "qwm", 2)
    qcast = DStream(nc, es, "qcast")
    qpj = DStream(nc, es, "qpj")
    qrow = DStream(nc, es, "qrow")
    qgar = DStream(nc, es, "qgar")
    qwl = DStream(nc, es, "qwl", 4)
    qws = DStream(nc, es, "qws", 4)
    qzd = DStream(nc, es, "qzd")
    qzb = DStream(nc, es, "qzb", 2)
    qwpt = DStream(nc, es, "qwpt", 2)
    qxr = DStream(nc, es, "qxr", 2)
    qx1s = DStream(nc, es, "qx1s", 2)
    qxqs = DStream(nc, es, "qxqs", 2)
    qxqf = DStream(nc, es, "qxqf")
    qwgu = DStream(nc, es, "qwgu")
    qh3s = DStream(nc, es, "qh3s", 2)
    qdwl = DStream(nc, es, "qdwl")
    qdws = DStream(nc, es, "qdws", 2)
    qh3r = DStream(nc, es, "qh3r", 2)
    qwdt = DStream(nc, es, "qwdt")
    qx1r = DStream(nc, es, "qx1r", 2)
    qso = DStream(nc, es, "qso", 2)
    qcc1 = DStream(nc, es, "qcc1")
    qmx1 = DStream(nc, es, "qmx1")
    qcc2 = DStream(nc, es, "qcc2")
    qmx2 = DStream(nc, es, "qmx2")
    qcoef = DStream(nc, es, "qcoef")
    qtap = DStream(nc, es, "qtap")

    ev = {}

    def gp_absmax(g, out_ap, in_ap):
        g.reduce_max(sm["reda"][:], in_ap, axis=AX.X)
        g.tensor_reduce(sm["redb"][:], in_ap, AX.X, ALU.min)
        return g.scalar_tensor_tensor(out_ap, sm["redb"][:], -1.0, sm["reda"][:],
                                      ALU.mult, ALU.max)

    x_tm = alloc("x_tm", [128, NT * C], F32)
    h_fm = alloc("h_fm", [128, NCC * N], BF16)
    w1t = alloc("w1t", [128, NCC * 3 * C], BF16)
    wm = [alloc(f"wm{i}", [128, 6 * C], BF16) for i in range(2)]
    hsc = [alloc(f"hsc{i}", [128, C], BF16) for i in range(2)]
    mvp = psall[:, 0:384]

    tv0 = tv.n
    for cc2 in range(NCC):
        ev[f"mv{cc2}"] = tv0 + cc2 + 1

    vv0 = vv.n
    for tc in range(NT):
        ev[f"st1{tc}"] = vv0 + tc + 1

    with nc.Block() as blk:
        @blk.sync
        def _(sync):
            qsm.inc(sync.dma_start(out=sm["ccol"][:], in_=crow.rearrange("(a p) -> p a", p=128), allow_slow_non_contiguous=True))
            for j, nm in [(0, "w0c"), (1, "w1p1c"), (2, "w2c")]:
                qsm.inc(sync.dma_start(out=sm[nm][:],
                               in_=wdwc[:, j:j + 1].rearrange("(a p) b -> p (a b)", p=128),
                               allow_slow_non_contiguous=True))
            qsm.inc(sync.dma_start(out=sm["bdc"][:], in_=bdwc.rearrange("(a p) -> p a", p=128), allow_slow_non_contiguous=True))
            qsm.inc(sync.dma_start(out=sm["bpc"][:], in_=bproj.rearrange("(a p) -> p a", p=128), allow_slow_non_contiguous=True))
            qsm.inc(sync.dma_start(out=sm["scal"][:, 8:9],
                           in_=clipx.rearrange("(a b) -> a b", b=1).unsqueeze(1).broadcast_to([1, 128, 1])))
            ev["smalls"] = qsm.inc(sync.dma_start(
                out=sm["scal"][:, 9:10],
                in_=clipd.rearrange("(a b) -> a b", b=1).unsqueeze(1).broadcast_to([1, 128, 1])))
            for tcx in range(NT):
                if tcx >= 4:
                    sync.wait_ge(vv.h, ev[f"st1{tcx - 4}"])
                ev[f"x{tcx}"] = qx.inc(sync.dma_start(
                    out=x_tm[:, tcx * C:(tcx + 1) * C],
                    in_=x_in.rearrange("(a p) c -> p a c", p=128)[:, tcx, :]))

        @blk.gpsimd
        def _(g):
            ev["wm0"] = qwm.inc(g.dma_start(out=wm[0][:], in_=wmod[0:128, :]))
            ev["wm1"] = qwm.inc(g.dma_start(out=wm[1][:], in_=wmod[128:256, :]))
            for qi in range(4):
                qcast.inc(g.dma_start(
                    out=wqkv_bf[qi * 768:(qi + 1) * 768, :],
                    in_=wqkv[qi * 768:(qi + 1) * 768, :]))
                for k in (2 * qi + 2, 2 * qi + 3):
                    if k < NCC:
                        g.wait_ge(tv.h, ev[f"mv{k - 2}"])
                        ev[f"wm{k}"] = qwm.inc(g.dma_start(
                            out=wm[k % 2][:], in_=wmod[k * 128:(k + 1) * 128, :]))
            ev["wprojbf"] = qpj.inc(g.dma_start(out=wproj_bf, in_=wproj))

        @blk.vector
        def _(v):
            v.memset(sm["epsln"][:], 1e-5)
            v.memset(sm["eps6"][:], 1e-6)
            v.memset(sm["rq"][:], 1.0 / 127.0)
            v.memset(sm["magic"][:], 12582912.0)
            for tc in range(NT):
                wq(v, ev[f"x{tc}"])
                v.bn_stats(sm["st6"][:, 12 * tc:12 * tc + 6], x_tm[:, tc * C:tc * C + 512])
                assert vv.inc(v.bn_stats(
                    sm["st6"][:, 12 * tc + 6:12 * tc + 12],
                    x_tm[:, tc * C + 512:(tc + 1) * C])) == ev[f"st1{tc}"]
            for tc in range(NT):
                v.wait_ge(vv.h, ev[f"st1{tc}"])
                mm = v.bn_aggr(sm["mv"][:, 2 * tc:2 * (tc + 1)], sm["st6"][:, 12 * tc:12 * (tc + 1)])
            ev["aggr"] = vv.inc(mm)

        @blk.scalar
        def _(s):
            wq(s, ev["smalls"])
            ev["siluc"] = sv.inc(s.activation(sm["siluc"][:], sm["ccol"][:], AF.Silu))
            s.wait_ge(vv.h, ev["aggr"])
            ev["sq_r1"] = sv.inc(s.activation(
                sm["r1t"][:], sm["mv"][:].rearrange("p (a b) -> p a b", b=2)[:, :, 1],
                AF.Sqrt, bias=sm["epsln"][:, 0:1]))

        @blk.vector
        def _(v):
            v.wait_ge(sv.h, ev["sq_r1"])
            evr = va.inc(v.reciprocal(sm["r1"][:], sm["r1t"][:]))
            v.wait_ge(va.h, evr)
            for tc in range(NT):
                mm = v.scalar_tensor_tensor(
                    sm["nmr1"][:, tc:tc + 1], sm["mv"][:, 2 * tc:2 * tc + 1],
                    -1.0, sm["r1"][:, tc:tc + 1], ALU.mult, ALU.mult)
            ev["rstd"] = vv.inc(mm)
            for tc in range(NT):
                if tc >= 2:
                    wq(v, qhs.tok(tc - 2))
                v.tensor_scalar(sm["scr2k"][:], x_tm[:, tc * C:(tc + 1) * C],
                                sm["r1"][:, tc:tc + 1], None, ALU.mult)
                ev[f"hv{tc}"] = vv.inc(v.tensor_scalar(
                    hsc[tc % 2][:], sm["scr2k"][:],
                    sm["nmr1"][:, tc:tc + 1], None, ALU.add))

        @blk.tensor
        def _(t):
            t.wait_ge(sv.h, ev["siluc"])
            for cc2 in range(NCC):
                wq(t, ev[f"wm{cc2}"])
                for oc in range(48):
                    mm = t.matmul(mvp[:, cc2 * 48 + oc: cc2 * 48 + oc + 1],
                                  wm[cc2 % 2][:, oc * 128:(oc + 1) * 128],
                                  sm["siluc"][:, cc2:cc2 + 1],
                                  start=True, stop=True, skip_group_check=True)
                assert tv.inc(mm) == ev[f"mv{cc2}"]

        @blk.vector
        def _(v):
            def dr(op):
                va.inc(op)
                v.wait_ge(va.h, va.n)
            v.wait_ge(tv.h, ev[f"mv{NCC - 1}"])
            dr(v.tensor_copy(sm["mod_cols"][:], mvp[:, 0:48]))
            for cc2 in range(1, NCC):
                dr(v.tensor_tensor(sm["mod_cols"][:], sm["mod_cols"][:],
                                   mvp[:, cc2 * 48:(cc2 + 1) * 48], ALU.add))
            wq(v, ev["smalls"])
            dr(v.tensor_scalar(sm["scr2k"][:, 0:8], sm["w1p1c"][:], 1.0, None, ALU.add))
            dr(v.tensor_copy(sm["w1p1c"][:], sm["scr2k"][:, 0:8]))
            mc = sm["mod_cols"]
            sa_s, ca_s, ga_s = mc[:, 0:8], mc[:, 8:16], mc[:, 16:24]
            sb_s, cb_s, gb_s = mc[:, 24:32], mc[:, 32:40], mc[:, 40:48]
            dr(v.tensor_scalar(sm["ca1c"][:], ca_s, 1.0, None, ALU.add))
            dr(v.reciprocal(sm["sapc"][:], sm["ca1c"][:]))
            dr(v.tensor_tensor(sm["sapc"][:], sa_s, sm["sapc"][:], ALU.mult))
            dr(v.tensor_scalar(sm["cb1c"][:], cb_s, 1.0, None, ALU.add))
            for src, dst in [(ga_s, "gac"), (gb_s, "gbc")]:
                dr(v.tensor_scalar(sm["sgn"][:], src, 0.0, None, ALU.is_ge))
                dr(v.tensor_scalar(sm["scr2k"][:, 8:16], sm["sgn"][:], 2.0, -1.0, ALU.mult, ALU.add))
                dr(v.tensor_copy(sm["sgn"][:], sm["scr2k"][:, 8:16]))
                dr(v.tensor_tensor(sm[dst][:], src, sm["sgn"][:], ALU.mult))
                dr(v.tensor_scalar(sm["scr2k"][:, 8:16], sm[dst][:], 0.25, None, ALU.max))
                dr(v.tensor_tensor(sm[dst][:], sm["scr2k"][:, 8:16], sm["sgn"][:], ALU.mult))
            ev["modc"] = vv.inc(v.tensor_tensor(sm["gabpc"][:], sm["gac"][:], sm["bpc"][:], ALU.mult))
            ev["ones"] = vv.inc(v.memset(sm["onescol"][:], 1.0))

        @blk.scalar
        def _(s):
            for tc in range(NT):
                s.wait_ge(vv.h, ev[f"hv{tc}"])
                ev[f"hsp{tc}"] = qhs.inc(s.dma_start(
                    out=h_dram.rearrange("(a p) c -> p a c", p=128)[:, tc, :],
                    in_=hsc[tc % 2][:]))

        vvd = vv.n
        for cc2 in range(NCC):
            ev[f"w1f{cc2}"] = vvd + 2 * cc2 + 1
            ev[f"hadd{cc2}"] = vvd + 2 * cc2 + 2

        @blk.sync
        def _(sync):
            wq_all(sync, qhs, qcast)
            for cc2 in range(NCC):
                if cc2 >= 4:
                    sync.wait_ge(vv.h, ev[f"hadd{cc2 - 4}"])
                ev[f"hfm{cc2}"] = qhf.inc(sync.dma_start(
                    out=h_fm[:, cc2 * N:(cc2 + 1) * N],
                    in_=h_dram[:, cc2 * 128:(cc2 + 1) * 128], transpose=True))
                if cc2 >= 4:
                    sync.wait_ge(vv.h, ev[f"w1f{cc2 - 4}"])
                ev[f"w1t{cc2}"] = qw1.inc(sync.dma_start(
                    out=w1t[:, cc2 * 3 * C:(cc2 + 1) * 3 * C],
                    in_=wqkv_bf[:, cc2 * 128:(cc2 + 1) * 128], transpose=True))

        @blk.vector
        def _(v):
            v.wait_ge(vv.h, ev["modc"])
            for cc2 in range(NCC):
                wq(v, ev[f"w1t{cc2}"])
                assert vv.inc(v.tensor_scalar(
                    w1t[:, cc2 * 3 * C:(cc2 + 1) * 3 * C],
                    w1t[:, cc2 * 3 * C:(cc2 + 1) * 3 * C],
                    sm["ca1c"][:, cc2:cc2 + 1], None, ALU.mult)) == ev[f"w1f{cc2}"]
                wq(v, ev[f"hfm{cc2}"])
                assert vv.inc(v.tensor_scalar(
                    h_fm[:, cc2 * N:(cc2 + 1) * N], h_fm[:, cc2 * N:(cc2 + 1) * N],
                    sm["sapc"][:, cc2:cc2 + 1], None, ALU.add)) == ev[f"hadd{cc2}"]

    with nc.Block() as blk:
        @blk.sync
        def _(sync):
            dws_ = []
            for r, colap in [(0, sm["gac"][:]), (1, sm["gabpc"][:]),
                             (2, sm["cb1c"][:]), (3, sm["mod_cols"][:, 24:32])]:
                dws_.append(sync.dma_start(
                    out=row_dram[r:r + 1, :].rearrange("a (b p) -> p (a b)", p=128),
                    in_=colap, allow_slow_non_contiguous=True))
            ev["rowsd"] = qrow.inc_group(dws_)
            wq(sync, ev["rowsd"])
            reps_ = [sync.dma_start(out=sm["gabp_rep"][:],
                           in_=row_dram[1:2, :].unsqueeze(1).broadcast_to([1, 128, C])),
                     sync.dma_start(out=sm["cb1_rep"][:],
                           in_=row_dram[2:3, :].unsqueeze(1).broadcast_to([1, 128, C])),
                     sync.dma_start(out=sm["sb_rep"][:],
                           in_=row_dram[3:4, :].unsqueeze(1).broadcast_to([1, 128, C]))]
            ev["reps"] = qrow.inc_group(reps_)

        @blk.gpsimd
        def _(g):
            wq(g, ev["rowsd"])
            ev["garep"] = qgar.inc(g.dma_start(
                out=sm["garep"][:],
                in_=row_dram[0:1, :].unsqueeze(1).broadcast_to([1, 128, C])))

    with nc.Block() as blk:
        @blk.sync
        def _(sync):
            wq_all(sync, qsm, qx, qhs, qhf, qw1, qwm, qcast, qpj, qrow, qgar)

    free("x_tm", "wm0", "wm1", "hsc0", "hsc1")

    if "h_fm" in taps or "mod" in taps:
        with nc.Block() as blk:
            @blk.sync
            def _(sync):
                if "h_fm" in taps:
                    qtap.inc(sync.dma_start(out=tap_outs["h_fm"], in_=h_fm[:]))
                if "mod" in taps:
                    qtap.inc(sync.dma_start(out=tap_outs["mod"], in_=sm["mod_cols"][:]))
                if "wqkvbf" in taps:
                    qtap.inc(sync.dma_start(out=tap_outs["wqkvbf"], in_=wqkv_bf))
                if "wprojbf" in taps:
                    wq(sync, ev["wprojbf"])
                    qtap.inc(sync.dma_start(out=tap_outs["wprojbf"], in_=wproj_bf))
                if "ca1c" in taps:
                    qtap.inc(sync.dma_start(out=tap_outs["ca1c"], in_=sm["ca1c"][:]))
                if "w1t" in taps:
                    for cc2 in range(NCC):
                        sync.wait_ge(vv.h, ev[f"w1f{cc2}"])
                    qtap.inc(sync.dma_start(out=tap_outs["w1t"], in_=w1t[:]))
                wq_all(sync, qtap)

    if stop_after == "p0":
        es.close()
        return nc, tap_outs

    a_fm = alloc("a_fm", [128, NCC * N], BF16)
    uq = [alloc(f"uq{i}", [128, 512], F32) for i in range(2)]
    sq2 = [alloc(f"sq2{i}", [128, 512], F32) for i in range(2)]
    psq = [bank(0), bank(1)]
    wnat = [alloc(f"wnat{i}", [128, C], F32) for i in range(4)]
    wqn = [alloc(f"wqn{i}", [128, C], I32) for i in range(2)]
    wqb = [alloc(f"wqb{i}", [128, C], BF16) for i in range(4)]

    def wsrc(k):
        return (wgate, wgq_dram, "sgc", "wginv") if k < NI else (wup, wuq_dram, "suc", "wuinv")

    for k in range(WPREP):
        ev[f"wpm{k}"] = k + 1
    for k in range(WPREP):
        ev[f"wl{k}"] = qwl.tok(k)
        ev[f"wsp{k}"] = qws.tok(k)

    wp_sc_next = [0]

    def emit_wprep_sc(s, budget):
        done = 0
        while done < budget and wp_sc_next[0] < WPREP:
            k = wp_sc_next[0]
            srct, dst, _, _ = wsrc(k)
            ic = k % NI
            if k >= 4:
                s.wait_ge(wpv.h, ev[f"wpm{k - 4}"])
            assert qwl.inc(s.dma_start(
                out=wnat[k % 4][:], in_=srct[ic * 128:(ic + 1) * 128, :])) == ev[f"wl{k}"]
            if k >= 2:
                kk = k - 2
                s.wait_ge(wpv.h, ev[f"wpm{kk}"])
                _, dstd, _, _ = wsrc(kk)
                assert qws.inc(s.dma_start(
                    out=dstd[(kk % NI) * 128:((kk % NI) + 1) * 128, :],
                    in_=wqb[kk % 4][:])) == ev[f"wsp{kk}"]
            wp_sc_next[0] += 1
            done += 1

    def emit_wprep_sc_tail(s):
        for kk in range(WPREP - 2, WPREP):
            s.wait_ge(wpv.h, ev[f"wpm{kk}"])
            _, dstd, _, _ = wsrc(kk)
            assert qws.inc(s.dma_start(
                out=dstd[(kk % NI) * 128:((kk % NI) + 1) * 128, :],
                in_=wqb[kk % 4][:])) == ev[f"wsp{kk}"]

    wp_next = [0]

    def emit_wprep_v(v, budget):
        done = 0
        while done < budget and wp_next[0] < WPREP:
            k = wp_next[0]
            _, _, snm, invnm = wsrc(k)
            ic = k % NI
            wq(v, ev[f"wl{k}"])
            evr = va.inc(v.reduce_max(sm["wred"][:], wnat[k % 4][:], axis=AX.X,
                         apply_absolute_value=True))
            v.wait_ge(va.h, evr)
            evr = va.inc(v.tensor_scalar(sm[snm][:, ic:ic + 1], sm["wred"][:], sm["rq"][:, 0:1], None, ALU.mult))
            v.wait_ge(va.h, evr)
            evr = va.inc(v.reciprocal(sm[invnm][:, ic:ic + 1], sm[snm][:, ic:ic + 1]))
            v.wait_ge(va.h, evr)
            for mh_ in range(2):
                msl_ = slice(mh_ * 512, (mh_ + 1) * 512)
                v.tensor_scalar(sm["scr2k"][:, 0:512], wnat[k % 4][:, msl_],
                                sm[invnm][:, ic:ic + 1], None, ALU.mult)
                v.tensor_scalar(sm["scr2k"][:, 512:1024], sm["scr2k"][:, 0:512],
                                sm["magic"][:, 0:1], None, ALU.add)
                v.tensor_scalar(wqn[k % 2][:, msl_], sm["scr2k"][:, 512:1024],
                                sm["magic"][:, 0:1], None, ALU.subtract)
            if k >= 4:
                wq(v, ev[f"wsp{k - 4}"])
            assert wpv.inc(v.tensor_scalar(
                wqb[k % 4][:], wqn[k % 2][:], QMAX, -QMAX - 1.0, ALU.min, ALU.max)) \
                == ev[f"wpm{k}"]
            wp_next[0] += 1
            done += 1

    if stop_after == "pw":
        with nc.Block() as blk:
            @blk.vector
            def _(v):
                emit_wprep_v(v, WPREP)

            @blk.scalar
            def _(s):
                emit_wprep_sc(s, WPREP)
                emit_wprep_sc_tail(s)
        es.close()
        return nc, tap_outs

    NQT = 32

    sv0 = sv.n
    for i in range(NQT):
        ev[f"qs{i}"] = sv0 + i + 1
    vv0 = vv.n
    pos = vv0
    for i in range(NQT):
        pos += 1
        ev[f"qu{i}"] = pos
        if i >= 1:
            pos += 1
            ev[f"qa{i - 1}"] = pos
    ev[f"qa{NQT - 1}"] = pos + 1

    with nc.Block() as blk:
        @blk.tensor
        def _(t):
            for cc2 in range(NCC):
                t.wait_ge(vv.h, ev[f"w1f{cc2}"])
                t.wait_ge(vv.h, ev[f"hadd{cc2}"])
            for i in range(NQT):
                oc, tg = divmod(i, 4)
                if i >= 2:
                    t.wait_ge(vv.h, ev[f"qu{i - 2}"])
                for cc2 in range(NCC):
                    mm = t.matmul(psq[i % 2][:],
                                  w1t[:, cc2 * 3 * C + oc * 128: cc2 * 3 * C + (oc + 1) * 128],
                                  h_fm[:, cc2 * N + tg * 512: cc2 * N + (tg + 1) * 512],
                                  start=(cc2 == 0), stop=(cc2 == NCC - 1))
                ev[f"qt{i}"] = tv.inc(mm)

        @blk.vector
        def _(v):
            for i in range(NQT):
                oc, tg = divmod(i, 4)
                v.wait_ge(tv.h, ev[f"qt{i}"])
                v.tensor_scalar(sq2[i % 2][:], psq[i % 2][:], 0.0, None, ALU.max)
                assert vv.inc(v.tensor_scalar(
                    uq[i % 2][:], sq2[i % 2][:], sm["eps6"][:, 0:1], None, ALU.add)) == ev[f"qu{i}"]
                if i >= 1:
                    j = i - 1
                    oj, tj = divmod(j, 4)
                    v.wait_ge(sv.h, ev[f"qs{j}"])
                    assert vv.inc(v.tensor_tensor(
                        a_fm[:, oj * N + tj * 512: oj * N + (tj + 1) * 512],
                        uq[j % 2][:], sq2[j % 2][:], ALU.mult)) == ev[f"qa{j}"]
                emit_wprep_v(v, 1 if i % 2 == 0 else 0)
            j = NQT - 1
            oj, tj = divmod(j, 4)
            v.wait_ge(sv.h, ev[f"qs{j}"])
            assert vv.inc(v.tensor_tensor(
                a_fm[:, oj * N + tj * 512: oj * N + (tj + 1) * 512],
                uq[j % 2][:], sq2[j % 2][:], ALU.mult)) == ev[f"qa{j}"]

        @blk.scalar
        def _(s):
            for i in range(NQT):
                s.wait_ge(vv.h, ev[f"qu{i}"])
                assert sv.inc(s.activation(sq2[i % 2][:], uq[i % 2][:], AF.Square)) == ev[f"qs{i}"]
                emit_wprep_sc(s, 1 if i % 2 == 0 else 0)

    if taps & {"a_fm", "w1tl", "hfml", "modl", "uq", "sq2", "qps", "hdrm"}:
        qps_s = alloc("qps_s", [128, 4096], F32) if "qps" in taps else None
        with nc.Block() as blk:
            if "qps" in taps:
                @blk.vector
                def _(v):
                    for jb in range(8):
                        mmv = v.tensor_copy(qps_s[:, jb * 512:(jb + 1) * 512],
                                            psall[:, jb * 512:(jb + 1) * 512])
                    ev["qpsd"] = vv.inc(mmv)

            @blk.sync
            def _(sync):
                if "qps" in taps:
                    sync.wait_ge(vv.h, ev["qpsd"])
                    qtap.inc(sync.dma_start(out=tap_outs["qps"], in_=qps_s[:]))
                if "a_fm" in taps:
                    qtap.inc(sync.dma_start(out=tap_outs["a_fm"], in_=a_fm[:]))
                if "uq" in taps:
                    qtap.inc(sync.dma_start(out=tap_outs["uq"], in_=uq[(NQT - 1) % 2][:]))
                if "sq2" in taps:
                    qtap.inc(sync.dma_start(out=tap_outs["sq2"], in_=sq2[(NQT - 1) % 2][:]))
                if "w1tl" in taps:
                    qtap.inc(sync.dma_start(out=tap_outs["w1tl"], in_=w1t[:]))
                if "hfml" in taps:
                    qtap.inc(sync.dma_start(out=tap_outs["hfml"], in_=h_fm[:]))
                if "modl" in taps:
                    qtap.inc(sync.dma_start(out=tap_outs["modl"], in_=sm["mod_cols"][:]))
                if "hdrm" in taps:
                    qtap.inc(sync.dma_start(out=tap_outs["hdrm"], in_=h_dram))
                wq_all(sync, qtap)
        if "qps" in taps:
            free("qps_s")

    if stop_after == "p1":
        if wp_next[0] < WPREP or wp_sc_next[0] < WPREP:
            with nc.Block() as blk:
                @blk.vector
                def _(v):
                    emit_wprep_v(v, WPREP)

                @blk.scalar
                def _(s):
                    emit_wprep_sc(s, WPREP)
                    emit_wprep_sc_tail(s)
        es.close()
        return nc, tap_outs

    ku = [alloc(f"ku{i}", [128, C], F32) for i in range(2)]
    ksq = [alloc(f"ksq{i}", [128, C], F32) for i in range(2)]
    k2b = [alloc(f"k2b{i}", [128, C], BF16) for i in range(2)]
    vbf = [alloc(f"vbf{i}", [128, C], BF16) for i in range(2)]
    swb = alloc("swb", [128, H], F32)
    psk = bank(0, 2)
    psv = bank(2, 2)
    kvacc = [bank(4), bank(5), bank(6)]

    def kvslot(p):
        return kvacc[p // 3], (p % 3) * 128, 384 + (p % 3)

    sv0 = sv.n
    for tc in range(NT):
        ev[f"kqs{tc}"] = sv0 + tc + 1
    tv0 = tv.n
    ev["kp0"], ev["vp0"] = tv0 + 1, tv0 + 2
    for tc in range(1, NT):
        ev[f"kp{tc}"] = tv0 + 3 * tc
        ev[f"vp{tc}"] = tv0 + 3 * tc + 1
        ev[f"pair{tc - 1}"] = tv0 + 3 * tc + 2
    ev[f"pair{NT - 1}"] = tv0 + 3 * NT

    with nc.Block() as blk:
        @blk.vector
        def _(v):
            v.wait_ge(vv.h, ev["ones"])
            for tc in range(NT):
                v.wait_ge(tv.h, ev[f"kp{tc}"])
                if tc >= 2:
                    v.wait_ge(tv.h, ev[f"pair{tc - 2}"])
                v.tensor_scalar(ksq[tc % 2][:], psk[:], 0.0, None, ALU.max)
                ev[f"ku{tc}"] = vv.inc(v.tensor_scalar(
                    ku[tc % 2][:], ksq[tc % 2][:], sm["eps6"][:, 0:1], None, ALU.add))
                v.wait_ge(sv.h, ev[f"kqs{tc}"])
                v.tensor_tensor(ksq[tc % 2][:], ku[tc % 2][:], ksq[tc % 2][:], ALU.mult)
                evr = va.inc(v.reduce_sum(swb[:], ksq[tc % 2][:].rearrange("p (a b) -> p a b", a=H), axis=AX.X))
                v.wait_ge(va.h, evr)
                evr = va.inc(v.reciprocal(sm["swr"][:], swb[:]))
                v.wait_ge(va.h, evr)
                for hh in range(H):
                    mm = v.tensor_scalar(k2b[tc % 2][:, hh * D:(hh + 1) * D],
                                         ksq[tc % 2][:, hh * D:(hh + 1) * D],
                                         sm["swr"][:, hh:hh + 1], None, ALU.mult)
                ev[f"k2{tc}"] = vv.inc(mm)
                v.wait_ge(tv.h, ev[f"vp{tc}"])
                ev[f"vcp{tc}"] = vv.inc(v.tensor_copy(vbf[tc % 2][:], psv[:]))
                emit_wprep_v(v, 1)

        @blk.scalar
        def _(s):
            for tc in range(NT):
                s.wait_ge(vv.h, ev[f"ku{tc}"])
                assert sv.inc(s.activation(ksq[tc % 2][:], ku[tc % 2][:], AF.Square)) == ev[f"kqs{tc}"]
                emit_wprep_sc(s, 1)

        @blk.tensor
        def _(t):
            for tc in range(NT):
                if tc >= 1:
                    t.wait_ge(vv.h, ev[f"ku{tc - 1}"])
                for j in range(2):
                    for cc2 in range(NCC):
                        mm = t.matmul(psk[:, j * 512:(j + 1) * 512],
                                      h_fm[:, cc2 * N + tc * 128: cc2 * N + (tc + 1) * 128],
                                      w1t[:, cc2 * 3 * C + C + j * 512: cc2 * 3 * C + C + (j + 1) * 512],
                                      start=(cc2 == 0), stop=(cc2 == NCC - 1))
                assert tv.inc(mm) == ev[f"kp{tc}"]
                if tc >= 1:
                    t.wait_ge(vv.h, ev[f"vcp{tc - 1}"])
                for j in range(2):
                    for cc2 in range(NCC):
                        mm = t.matmul(psv[:, j * 512:(j + 1) * 512],
                                      h_fm[:, cc2 * N + tc * 128: cc2 * N + (tc + 1) * 128],
                                      w1t[:, cc2 * 3 * C + 2 * C + j * 512: cc2 * 3 * C + 2 * C + (j + 1) * 512],
                                      start=(cc2 == 0), stop=(cc2 == NCC - 1))
                assert tv.inc(mm) == ev[f"vp{tc}"]
                if tc >= 1:
                    tcm = tc - 1
                    t.wait_ge(vv.h, ev[f"k2{tcm}"])
                    t.wait_ge(vv.h, ev[f"vcp{tcm}"])
                    for p in range(NCC):
                        bank, off, kcol = kvslot(p)
                        t.matmul(bank[:, off:off + 128],
                                 k2b[tcm % 2][:, p * 128:(p + 1) * 128],
                                 vbf[tcm % 2][:, p * 128:(p + 1) * 128],
                                 start=(tcm == 0 and p % 3 == 0),
                                 stop=(tcm == NT - 1), skip_group_check=True)
                        mm = t.matmul(bank[:, kcol:kcol + 1],
                                      k2b[tcm % 2][:, p * 128:(p + 1) * 128],
                                      sm["onescol"][:],
                                      start=False, stop=(tcm == NT - 1), skip_group_check=True)
                    assert tv.inc(mm) == ev[f"pair{tcm}"]
            tcm = NT - 1
            t.wait_ge(vv.h, ev[f"k2{tcm}"])
            t.wait_ge(vv.h, ev[f"vcp{tcm}"])
            for p in range(NCC):
                bank, off, kcol = kvslot(p)
                t.matmul(bank[:, off:off + 128],
                         k2b[tcm % 2][:, p * 128:(p + 1) * 128],
                         vbf[tcm % 2][:, p * 128:(p + 1) * 128],
                         start=False, stop=True, skip_group_check=True)
                mm = t.matmul(bank[:, kcol:kcol + 1],
                              k2b[tcm % 2][:, p * 128:(p + 1) * 128],
                              sm["onescol"][:],
                              start=False, stop=True, skip_group_check=True)
            assert tv.inc(mm) == ev[f"pair{tcm}"]

    if "kps" in taps:
        kps_i0 = qtap.i
        with nc.Block() as blk:
            @blk.vector
            def _(v):
                for rnd in range(2):
                    if rnd:
                        wq(v, qtap.tok(kps_i0))
                    v.tensor_copy(sm["scr2k"][:, 0:512], psall[:, rnd * 1024:rnd * 1024 + 512])
                    ev[f"kpsd{rnd}"] = vv.inc(v.tensor_copy(
                        sm["scr2k"][:, 512:1024], psall[:, rnd * 1024 + 512:rnd * 1024 + 1024]))

            @blk.sync
            def _(sync):
                for rnd in range(2):
                    sync.wait_ge(vv.h, ev[f"kpsd{rnd}"])
                    evd = qtap.inc(sync.dma_start(
                        out=tap_outs["kps"][:, rnd * 1024:(rnd + 1) * 1024],
                        in_=sm["scr2k"][:]))
                    assert evd == qtap.tok(kps_i0 + rnd)
                wq_all(sync, qtap)

    if "lnst" in taps:
        with nc.Block() as blk:
            @blk.sync
            def _(sync):
                qtap.inc_group([
                    sync.dma_start(out=tap_outs["lnst"][:, 0:32], in_=sm["mv"][:]),
                    sync.dma_start(out=tap_outs["lnst"][:, 32:48], in_=sm["r1"][:]),
                    sync.dma_start(out=tap_outs["lnst"][:, 48:64], in_=sm["nmr1"][:])])
                wq_all(sync, qtap)

    if taps & {"k2", "ku", "swr"}:
        with nc.Block() as blk:
            @blk.sync
            def _(sync):
                if "k2" in taps:
                    qtap.inc(sync.dma_start(out=tap_outs["k2"], in_=k2b[(NT - 1) % 2][:]))
                if "ku" in taps:
                    qtap.inc(sync.dma_start(out=tap_outs["ku"], in_=ku[(NT - 1) % 2][:]))
                if "swr" in taps:
                    qtap.inc(sync.dma_start(out=tap_outs["swr"], in_=sm["swr"][:]))
                wq_all(sync, qtap)

    free("h_fm", "w1t", "uq0", "uq1", "sq20", "sq21", "ku0", "ku1")

    if stop_after == "p2":
        if wp_next[0] < WPREP or wp_sc_next[0] < WPREP:
            with nc.Block() as blk:
                @blk.vector
                def _(v):
                    emit_wprep_v(v, WPREP)

                @blk.scalar
                def _(s):
                    emit_wprep_sc(s, WPREP)
                    emit_wprep_sc_tail(s)
        es.close()
        return nc, tap_outs

    kvst = alloc("kvst", [128, NCC * 128], BF16)
    ksmat = alloc("ksmat", [128, NCC * 16], BF16)

    with nc.Block() as blk:
        @blk.vector
        def _(v):
            va.inc(v.memset(ksmat[:], 0.0))
            va.inc(v.memset(kvst[:], 0.0))
            v.wait_ge(va.h, va.n)
            v.wait_ge(tv.h, ev[f"pair{NT - 1}"])
            for p in range(NCC):
                bank, off, kcol = kvslot(p)
                v.tensor_scalar(ksmat[0:64, p * 16 + 2 * p: p * 16 + 2 * p + 1],
                                bank[0:64, kcol:kcol + 1], sm["eps6"][0:64, 0:1], None, ALU.add)
                v.tensor_scalar(ksmat[64:128, p * 16 + 2 * p + 1: p * 16 + 2 * p + 2],
                                bank[64:128, kcol:kcol + 1], sm["eps6"][64:128, 0:1], None, ALU.add)
                v.tensor_copy(kvst[0:64, p * 128: p * 128 + 64], bank[0:64, off:off + 64])
                mm = v.tensor_copy(kvst[64:128, p * 128 + 64: p * 128 + 128],
                                   bank[64:128, off + 64:off + 128])
            ev["kvst"] = vv.inc(mm)

    free("ksq0", "ksq1", "k2b0", "k2b1", "vbf0", "vbf1", "swb")

    if "kvst" in taps or "ksmat" in taps:
        with nc.Block() as blk:
            @blk.sync
            def _(sync):
                if "kvst" in taps:
                    qtap.inc(sync.dma_start(out=tap_outs["kvst"], in_=kvst[:]))
                if "ksmat" in taps:
                    qtap.inc(sync.dma_start(out=tap_outs["ksmat"], in_=ksmat[:]))
                wq_all(sync, qtap)

    conv_fm = [alloc(f"conv_fm{i}", [128, N], BF16) for i in range(NCC)]
    wpt = alloc("wpt", [128, NCC * C], BF16)
    zbb = alloc("zbb", [16, N], BF16)
    zbc = [alloc(f"zbc{i}", [128, N], BF16) for i in range(2)]
    ycur = [alloc(f"ycur{i}", [128, N], F32) for i in range(2)]
    cv1 = [alloc(f"cv1{i}", [128, N], F32) for i in range(2)]
    psden = psall[0:16, 0:2048]
    psy = [bank(4), bank(5)]
    tv0 = tv.n
    ev["den"] = tv0 + 1
    for i in range(NCC * 4):
        ev[f"ya{i}"] = tv0 + 2 + i
    for cc2 in range(NCC):
        ev[f"zbc{cc2}"] = (qzb.sems[cc2 % 2].h, 32 * (cc2 // 2 + 1))

    with nc.Block() as blk:
        @blk.vector
        def _(v):
            v.wait_ge(tv.h, ev["den"])
            with nc.allow_low_precision(reason="z applied in bf16 (validated margin)"):
                ev["z"] = va.inc(v.reciprocal(zbb[:], psden[:]))
            for cc2 in range(NCC):
                wq(v, ev[f"zbc{cc2}"])
                ev[f"ap{cc2}"] = vv.inc(v.tensor_tensor(
                    a_fm[:, cc2 * N:(cc2 + 1) * N], a_fm[:, cc2 * N:(cc2 + 1) * N],
                    zbc[cc2 % 2][:], ALU.mult))
            for cc2 in range(NCC):
                for g4 in range(4):
                    i = cc2 * 4 + g4
                    v.wait_ge(tv.h, ev[f"ya{i}"])
                    ev[f"yevt{i}"] = vv.inc(v.tensor_copy(
                        ycur[cc2 % 2][:, g4 * 512:(g4 + 1) * 512], psy[i % 2][:]))
                for hf in range(2):
                    v.tensor_scalar(sm["scr2k"][:], ycur[cc2 % 2][:, hf * C:(hf + 1) * C],
                                    sm["w1p1c"][:, cc2:cc2 + 1], None, ALU.mult)
                    v.tensor_scalar(cv1[cc2 % 2][:, hf * C:(hf + 1) * C], sm["scr2k"][:],
                                    sm["bdc"][:, cc2:cc2 + 1], None, ALU.add)
                v.scalar_tensor_tensor(cv1[cc2 % 2][:, 1:N], ycur[cc2 % 2][:, 0:N - 1],
                                       sm["w0c"][:, cc2:cc2 + 1], cv1[cc2 % 2][:, 1:N],
                                       ALU.mult, ALU.add)
                v.scalar_tensor_tensor(conv_fm[cc2][:, 0: N - 1],
                                       ycur[cc2 % 2][:, 1:N],
                                       sm["w2c"][:, cc2:cc2 + 1], cv1[cc2 % 2][:, 0:N - 1],
                                       ALU.mult, ALU.add)
                ev[f"cv{cc2}"] = vv.inc(v.tensor_copy(
                    conv_fm[cc2][:, N - 1: N], cv1[cc2 % 2][:, N - 1:N]))
                emit_wprep_v(v, 2)
            emit_wprep_v(v, WPREP)

        @blk.tensor
        def _(t):
            t.wait_ge(vv.h, ev["kvst"])
            for cc2 in range(NCC):
                for tg in range(4):
                    mm = t.matmul(psden[:, tg * 512:(tg + 1) * 512],
                                  ksmat[:, cc2 * 16:(cc2 + 1) * 16],
                                  a_fm[:, cc2 * N + tg * 512: cc2 * N + (tg + 1) * 512],
                                  start=(cc2 == 0), stop=(cc2 == NCC - 1),
                                  skip_group_check=True)
            assert tv.inc(mm) == ev["den"]
            for cc2 in range(NCC):
                t.wait_ge(vv.h, ev[f"ap{cc2}"])
                for g4 in range(4):
                    i = cc2 * 4 + g4
                    if i >= 2:
                        t.wait_ge(vv.h, ev[f"yevt{i - 2}"])
                    assert tv.inc(t.matmul(
                        psy[i % 2][:], kvst[:, cc2 * 128:(cc2 + 1) * 128],
                        a_fm[:, cc2 * N + g4 * 512: cc2 * N + (g4 + 1) * 512],
                        start=True, stop=True)) == ev[f"ya{i}"]

        @blk.scalar
        def _(s):
            emit_wprep_sc(s, WPREP)
            emit_wprep_sc_tail(s)

        @blk.sync
        def _(sync):
            sync.wait_ge(va.h, ev["z"])
            zd = qzd.inc(sync.dma_start(out=z_dram, in_=zbb[:]))
            wq(sync, zd)
            for cc2 in range(NCC):
                if cc2 >= 2:
                    sync.wait_ge(vv.h, ev[f"ap{cc2 - 2}"])
                tokz = qzb.inc_group([
                    sync.dma_start(out=zbc[cc2 % 2][0:64, :],
                                   in_=z_dram[2 * cc2:2 * cc2 + 1, :].unsqueeze(1).broadcast_to([1, 64, N])),
                    sync.dma_start(out=zbc[cc2 % 2][64:128, :],
                                   in_=z_dram[2 * cc2 + 1:2 * cc2 + 2, :].unsqueeze(1).broadcast_to([1, 64, N]))])
                assert tokz == ev[f"zbc{cc2}"]


    vvw = vv.n
    for cc2 in range(NCC):
        ev[f"wpf{cc2}"] = vvw + cc2 + 1
    with nc.Block() as blk:
        @blk.sync
        def _(sync):
            wq(sync, ev["wprojbf"])
            for cc2 in range(NCC):
                if cc2 >= 2:
                    sync.wait_ge(vv.h, ev[f"wpf{cc2 - 2}"])
                ev[f"wptl{cc2}"] = qwpt.inc(sync.dma_start(
                    out=wpt[:, cc2 * C:(cc2 + 1) * C],
                    in_=wproj_bf[:, cc2 * 128:(cc2 + 1) * 128], transpose=True))

        @blk.vector
        def _(v):
            wq(v, ev["garep"])
            for cc2 in range(NCC):
                wq(v, ev[f"wptl{cc2}"])
                assert vv.inc(v.tensor_tensor(
                    wpt[:, cc2 * C:(cc2 + 1) * C], wpt[:, cc2 * C:(cc2 + 1) * C],
                    sm["garep"][:], ALU.mult)) == ev[f"wpf{cc2}"]

    free("kvst", "ksmat", "zbb", "zbc0", "zbc1", "a_fm")

    if "z" in taps or "conv" in taps:
        with nc.Block() as blk:
            @blk.sync
            def _(sync):
                if "z" in taps:
                    qtap.inc(sync.dma_start(out=tap_outs["z"], in_=z_dram))
                if "conv" in taps:
                    for i in range(NCC):
                        qtap.inc(sync.dma_start(
                            out=tap_outs["conv"][:, i * N:(i + 1) * N], in_=conv_fm[i][:]))
                wq_all(sync, qtap)

    if stop_after == "p3":
        if wp_next[0] < WPREP or wp_sc_next[0] < WPREP:
            with nc.Block() as blk:
                @blk.vector
                def _(v):
                    emit_wprep_v(v, WPREP)

                @blk.scalar
                def _(s):
                    emit_wprep_sc(s, WPREP)
                    emit_wprep_sc_tail(s)
        es.close()
        return nc, tap_outs

    h2r = [alloc(f"h2r{i}", [128, C], F16) for i in range(NT)]
    xrl = [alloc(f"xrl{i}", [128, C], F32) for i in range(2)]
    x1c = [alloc(f"x1c{i}", [128, C], F32) for i in range(2)]
    t2s = [alloc(f"t2s{i}", [128, C], F32) for i in range(2)]
    hms = [alloc(f"hms{i}", [128, C], F32) for i in range(2)]
    st2 = alloc("st2", [128, 12], F32)
    mv2 = alloc("mv2", [128, 2], F32)
    r2c = alloc("r2c", [128, NT], F32)
    nmr2 = alloc("nmr2", [128, NT], F32)
    psx1 = [bank(0, 2), bank(2, 2)]

    vv0 = vv.n
    for tc in range(NT):
        ev[f"xr2{tc}"] = vv0 + 6 * tc + 1
        ev[f"x1{tc}"] = vv0 + 6 * tc + 2
        ev[f"st2b{tc}"] = vv0 + 6 * tc + 3
        ev[f"ag2{tc}"] = vv0 + 6 * tc + 4
        ev[f"t2v{tc}"] = vv0 + 6 * tc + 5
        ev[f"h2{tc}"] = vv0 + 6 * tc + 6
    sv0 = sv.n
    for tc in range(NT):
        ev[f"r2{tc}"] = sv0 + tc + 1
    for tc in range(NT):
        ev[f"x1s{tc}"] = qx1s.tok(tc)
        ev[f"xrl{tc}"] = qxr.tok(tc)

    with nc.Block() as blk:
        @blk.sync
        def _(sync):
            for tc in range(NT):
                if tc >= 2:
                    sync.wait_ge(vv.h, ev[f"x1{tc - 2}"])
                assert qxr.inc(sync.dma_start(
                    out=xrl[tc % 2][:],
                    in_=x_in.rearrange("(a p) c -> p a c", p=128)[:, tc, :])) == ev[f"xrl{tc}"]

        @blk.tensor
        def _(t):
            for cc2 in range(NCC):
                t.wait_ge(vv.h, ev[f"cv{cc2}"])
                t.wait_ge(vv.h, ev[f"wpf{cc2}"])
            for tc in range(NT):
                if tc >= 2:
                    t.wait_ge(vv.h, ev[f"x1{tc - 2}"])
                for cc2 in range(NCC):
                    for j in range(2):
                        mm = t.matmul(psx1[tc % 2][:, j * 512:(j + 1) * 512],
                                      conv_fm[cc2][:, tc * 128:(tc + 1) * 128],
                                      wpt[:, cc2 * C + j * 512: cc2 * C + (j + 1) * 512],
                                      start=(cc2 == 0), stop=(cc2 == NCC - 1),
                                      skip_group_check=True)
                ev[f"pj{tc}"] = tv.inc(mm)

        @blk.vector
        def _(v):
            wq(v, ev["reps"])
            for tc in range(NT):
                wq(v, ev[f"xrl{tc}"])
                assert vv.inc(v.tensor_tensor(
                    xrl[tc % 2][:], xrl[tc % 2][:], sm["gabp_rep"][:], ALU.add)) == ev[f"xr2{tc}"]
                v.wait_ge(tv.h, ev[f"pj{tc}"])
                if tc >= 2:
                    wq(v, ev[f"x1s{tc - 2}"])
                assert vv.inc(v.tensor_tensor(
                    x1c[tc % 2][:], psx1[tc % 2][:], xrl[tc % 2][:], ALU.add)) == ev[f"x1{tc}"]
                v.bn_stats(st2[:, 0:6], x1c[tc % 2][:, 0:512])
                assert vv.inc(v.bn_stats(st2[:, 6:12], x1c[tc % 2][:, 512:C])) == ev[f"st2b{tc}"]
                v.wait_ge(vv.h, ev[f"st2b{tc}"])
                assert vv.inc(v.bn_aggr(mv2[:], st2[:])) == ev[f"ag2{tc}"]
                v.wait_ge(sv.h, ev[f"r2{tc}"])
                evr = va.inc(v.reciprocal(r2c[:, tc:tc + 1], sm["r2t"][:]))
                v.wait_ge(va.h, evr)
                v.scalar_tensor_tensor(
                    nmr2[:, tc:tc + 1], mv2[:, 0:1], -1.0, r2c[:, tc:tc + 1],
                    ALU.mult, ALU.mult)
                v.tensor_scalar(sm["scr2k"][:], x1c[tc % 2][:],
                                r2c[:, tc:tc + 1], None, ALU.mult)
                assert vv.inc(v.tensor_scalar(
                    t2s[tc % 2][:], sm["scr2k"][:],
                    nmr2[:, tc:tc + 1], None, ALU.add)) == ev[f"t2v{tc}"]
                v.tensor_tensor(hms[tc % 2][:], t2s[tc % 2][:], sm["cb1_rep"][:], ALU.mult)
                assert vv.inc(v.tensor_tensor(
                    h2r[tc][:], hms[tc % 2][:], sm["sb_rep"][:],
                    ALU.add)) == ev[f"h2{tc}"]
                ev[f"gm{tc}"] = gv.inc(v.reduce_max(
                    sm["gmx"][:, tc:tc + 1], h2r[tc][:],
                    axis=AX.X, apply_absolute_value=True))

        @blk.scalar
        def _(s):
            for tc in range(NT):
                s.wait_ge(vv.h, ev[f"ag2{tc}"])
                assert sv.inc(s.activation(sm["r2t"][:], mv2[:, 1:2],
                                           AF.Sqrt, bias=sm["epsln"][:, 0:1])) == ev[f"r2{tc}"]
                s.wait_ge(vv.h, ev[f"t2v{tc}"])
                assert qx1s.inc(s.dma_start(
                    out=x1_dram.rearrange("(a p) c -> p a c", p=128)[:, tc, :],
                    in_=x1c[tc % 2][:])) == ev[f"x1s{tc}"]

    free(*[f"conv_fm{i}" for i in range(NCC)], "wpt", "ycur0", "ycur1", "cv10", "cv11",
         "t2s0", "t2s1", "hms0", "hms1",
         "wnat0", "wnat1", "wnat2", "wnat3", "wqn0", "wqn1",
         "wqb0", "wqb1", "wqb2", "wqb3")

    if "x1" in taps or "h2" in taps:
        with nc.Block() as blk:
            @blk.sync
            def _(sync):
                if "x1" in taps:
                    wq_all(sync, qx1s)
                    qtap.inc(sync.dma_start(out=tap_outs["x1"], in_=x1_dram))
                if "h2" in taps:
                    for i in range(NT):
                        qtap.inc(sync.dma_start(
                            out=tap_outs["h2"][:, i * C:(i + 1) * C], in_=h2r[i][:]))
                wq_all(sync, qtap)

    if stop_after == "p4":
        if wp_next[0] < WPREP or wp_sc_next[0] < WPREP:
            with nc.Block() as blk:
                @blk.vector
                def _(v):
                    emit_wprep_v(v, WPREP)

                @blk.scalar
                def _(s):
                    emit_wprep_sc(s, WPREP)
                    emit_wprep_sc_tail(s)
        es.close()
        return nc, tap_outs

    wgt = [alloc(f"wgt{i}", [128, II], BF16) for i in range(NCC)]
    wut = [alloc(f"wut{i}", [128, II], BF16) for i in range(NCC)]
    xq_fm = [alloc(f"xq_fm{i}", [128, N], BF16) for i in range(NCC)]
    q32 = [alloc(f"q32{i}", [128, C], I32) for i in range(2)]
    xqc = [alloc(f"xqc{i}", [128, C], BF16) for i in range(2)]

    with nc.Block() as blk:
        @blk.vector
        def _(v):
            for tc in range(NT):
                v.wait_ge(gv.h, ev[f"gm{tc}"])
            ev["lmax1"] = vv.inc(v.reduce_max(sm["scal"][:, 0:1], sm["gmx"][:], axis=AX.X))

        @blk.gpsimd
        def _(g):
            g.wait_ge(vv.h, ev["lmax1"])
            ev["cc1i"] = qcc1.inc(g.dma_start(
                out=cc1_in.ap().rearrange("a (b p) -> p (a b)", p=128),
                in_=sm["scal"][:, 0:1], allow_slow_non_contiguous=True))
            wq(g, ev["cc1i"])
            ev["cc1"] = ccs.inc(g.collective_compute(
                "AllReduce", ALU.max, replica_groups=[list(range(8))],
                ins=[cc1_in.ap().opt()], outs=[cc1_out.ap().opt()]))
            g.wait_ge(ccs.h, ev["cc1"])
            ev["mx1r"] = qcc1.inc(g.dma_start(out=sm["tinyrow"][:], in_=cc1_out.ap()))

        @blk.vector
        def _(v):
            wq(v, ev["mx1r"])
            ev["mx1"] = vv.inc(v.reduce_max(sm["tiny1"][:], sm["tinyrow"][:], axis=AX.X))

        @blk.sync
        def _(sync):
            sync.wait_ge(vv.h, ev["mx1"])
            ev["mx1d"] = qmx1.inc(sync.dma_start(
                out=vec_dram[0:1, 0:1], in_=sm["tiny1"][:]))
            wq(sync, ev["mx1d"])
            ev["gmax1"] = qmx1.inc(sync.dma_start(
                out=sm["scal"][:, 0:1],
                in_=vec_dram[0:1, 0:1].unsqueeze(1).broadcast_to([1, 128, 1])))

        @blk.scalar
        def _(s):
            wq(s, ev["gmax1"])
            ev["sigx"] = sv.inc(s.activation(sm["scal"][:, 6:7], sm["scal"][:, 8:9], AF.Sigmoid))

    for tc in range(NT):
        ev[f"xqs{tc}"] = qxqs.tok(tc)

    with nc.Block() as blk:
        @blk.vector
        def _(v):
            v.wait_ge(sv.h, ev["sigx"])
            evr = va.inc(v.tensor_tensor(sm["scal"][:, 10:11], sm["scal"][:, 0:1], sm["scal"][:, 6:7], ALU.mult))
            v.wait_ge(va.h, evr)
            evr = va.inc(v.tensor_scalar(sm["scal"][:, 1:2], sm["scal"][:, 10:11], sm["rq"][:, 0:1], None, ALU.mult))
            v.wait_ge(va.h, evr)
            ev["s1"] = va.inc(v.reciprocal(sm["scal"][:, 2:3], sm["scal"][:, 1:2]))
            v.wait_ge(va.h, ev["s1"])
            v.tensor_scalar(sm["sgatec"][:], sm["sgc"][:], sm["scal"][:, 1:2], None, ALU.mult)
            ev["scols"] = vv.inc(v.tensor_scalar(
                sm["supc"][:], sm["suc"][:], sm["scal"][:, 1:2], None, ALU.mult))
            for tc in range(NT):
                if tc >= 2:
                    wq(v, ev[f"xqs{tc - 2}"])
                for mh_ in range(2):
                    msl_ = slice(mh_ * 512, (mh_ + 1) * 512)
                    v.tensor_scalar(sm["scr2k"][:, 0:512], h2r[tc][:, msl_],
                                    sm["scal"][:, 2:3], None, ALU.mult)
                    v.tensor_scalar(sm["scr2k"][:, 512:1024], sm["scr2k"][:, 0:512],
                                    sm["magic"][:, 0:1], None, ALU.add)
                    v.tensor_scalar(q32[tc % 2][:, msl_], sm["scr2k"][:, 512:1024],
                                    sm["magic"][:, 0:1], None, ALU.subtract)
                ev[f"xq{tc}"] = vv.inc(v.tensor_scalar(
                    xqc[tc % 2][:], q32[tc % 2][:], QMAX, -QMAX - 1.0, ALU.min, ALU.max))

        @blk.sync
        def _(sync):
            for tc in range(NT):
                sync.wait_ge(vv.h, ev[f"xq{tc}"])
                assert qxqs.inc(sync.dma_start(
                    out=xq_dram.rearrange("(a p) c -> p a c", p=128)[:, tc, :],
                    in_=xqc[tc % 2][:])) == ev[f"xqs{tc}"]
            for hc in range(NCC):
                qxqf.inc(sync.dma_start(
                    out=xq_fm[hc][:],
                    in_=xq_dram[:, hc * 128:(hc + 1) * 128], transpose=True))

        @blk.scalar
        def _(s):
            wq(s, ev[f"wsp{WPREP - 1}"])
            wq(s, ev[f"wsp{WPREP - 2}"])
            for hc in range(NCC):
                qwgu.inc(s.dma_start(
                    out=wgt[hc][:],
                    in_=wgq_dram[:, hc * 128:(hc + 1) * 128], transpose=True))
            for hc in range(NCC):
                qwgu.inc(s.dma_start(
                    out=wut[hc][:],
                    in_=wuq_dram[:, hc * 128:(hc + 1) * 128], transpose=True))

    free(*[f"h2r{i}" for i in range(NT)], "q320", "q321", "xqc0", "xqc1", "xrl0", "xrl1", "x1c0", "x1c1",
         "st2", "mv2", "r2c", "nmr2")

    if taps & {"xq", "s1", "wgq"}:
        with nc.Block() as blk:
            @blk.sync
            def _(sync):
                if "xq" in taps:
                    qtap.inc(sync.dma_start(out=tap_outs["xq"], in_=xq_dram))
                if "s1" in taps:
                    qtap.inc(sync.dma_start(out=tap_outs["s1"], in_=sm["scal"][:, 1:2]))
                if "wgq" in taps:
                    qtap.inc(sync.dma_start(out=tap_outs["wgq"], in_=wgq_dram))
                wq_all(sync, qtap)

    if stop_after == "p5":
        if wp_next[0] < WPREP or wp_sc_next[0] < WPREP:
            with nc.Block() as blk:
                @blk.vector
                def _(v):
                    emit_wprep_v(v, WPREP)

                @blk.scalar
                def _(s):
                    emit_wprep_sc(s, WPREP)
                    emit_wprep_sc_tail(s)
        es.close()
        return nc, tap_outs

    gbuf = [alloc(f"gbuf{i}", [128, 512], F32) for i in range(2)]
    h3b = [alloc(f"h3b{i}", [128, N], F32) for i in range(2)]
    wnd = alloc("wnd", [128, II], F32)
    wqnd = alloc("wqnd", [128, II // 2], I32)
    wqbd = [alloc(f"wqbd{i}", [128, II], BF16) for i in range(2)]
    psg = [bank(0), bank(1)]
    psu = [bank(2), bank(3)]

    for k in range(8):
        ev[f"dwl{k}"] = qdwl.tok(k)
        ev[f"dws{k}"] = qdws.tok(k)
    dpg0 = dpg.n
    for k in range(8):
        ev[f"dwc{k}"] = dpg0 + k + 1

    dw_next = [0]

    def emit_down_v(v):
        k = dw_next[0]
        if k >= 8:
            return
        HW = II // 2
        wq(v, ev[f"dwl{k}"])
        evr = va.inc(v.reduce_max(sm["dred"][:], wnd[:], axis=AX.X, apply_absolute_value=True))
        v.wait_ge(va.h, evr)
        evr = va.inc(v.tensor_scalar(sm["sdc"][:, k:k + 1], sm["dred"][:], sm["rq"][:, 0:1], None, ALU.mult))
        v.wait_ge(va.h, evr)
        evr = va.inc(v.reciprocal(sm["wdinv"][:, k:k + 1], sm["sdc"][:, k:k + 1]))
        v.wait_ge(va.h, evr)
        if k >= 2:
            wq(v, ev[f"dws{k - 2}"])
        for half in range(2):
            for qq_ in range(4):
                s0 = half * HW + qq_ * 352
                v.tensor_scalar(sm["scr2k"][:, 0:352], wnd[:, s0:s0 + 352],
                                sm["wdinv"][:, k:k + 1], None, ALU.mult)
                v.tensor_scalar(sm["scr2k"][:, 512:864], sm["scr2k"][:, 0:352],
                                sm["magic"][:, 0:1], None, ALU.add)
                v.tensor_scalar(wqnd[:, qq_ * 352:(qq_ + 1) * 352], sm["scr2k"][:, 512:864],
                                sm["magic"][:, 0:1], None, ALU.subtract)
            mmv = v.tensor_scalar(
                wqbd[k % 2][:, half * HW:(half + 1) * HW], wqnd[:],
                QMAX, -QMAX - 1.0, ALU.min, ALU.max)
        assert dpg.inc(mmv) == ev[f"dwc{k}"]
        dw_next[0] += 1

    sv0 = sv.n
    for i4 in range(NI * 4):
        ev[f"g{i4}"] = sv0 + i4 + 1
    vv0 = vv.n
    for i4 in range(NI * 4):
        ev[f"h3{i4}"] = vv0 + i4 + 1
    for ic in range(NI):
        ev[f"h3s{ic}"] = qh3s.tok(ic)

    with nc.Block() as blk:
        @blk.tensor
        def _(t):
            wq_all(t, qxqf, qwgu)
            t.wait_ge(vv.h, ev["scols"])
            for ic in range(NI):
                for tg in range(4):
                    i4 = ic * 4 + tg
                    if i4 >= 2:
                        t.wait_ge(sv.h, ev[f"g{i4 - 2}"])
                    for hc in range(NCC):
                        mm = t.matmul(psg[i4 % 2][:],
                                      wgt[hc][:, ic * 128:(ic + 1) * 128],
                                      xq_fm[hc][:, tg * 512:(tg + 1) * 512],
                                      start=(hc == 0), stop=(hc == NCC - 1))
                    ev[f"gmm{i4}"] = tv.inc(mm)
                    if i4 >= 2:
                        t.wait_ge(vv.h, ev[f"h3{i4 - 2}"])
                    for hc in range(NCC):
                        mm = t.matmul(psu[i4 % 2][:],
                                      wut[hc][:, ic * 128:(ic + 1) * 128],
                                      xq_fm[hc][:, tg * 512:(tg + 1) * 512],
                                      start=(hc == 0), stop=(hc == NCC - 1))
                    ev[f"umm{i4}"] = tv.inc(mm)

        @blk.vector
        def _(v):
            for ic in range(NI):
                for tg in range(4):
                    i4 = ic * 4 + tg
                    if tg == 0 and ic >= 2:
                        wq(v, ev[f"h3s{ic - 2}"])
                        v.wait_ge(gv.h, ev[f"hm{ic - 2}"])
                    v.wait_ge(tv.h, ev[f"umm{i4}"])
                    v.wait_ge(sv.h, ev[f"g{i4}"])
                    assert vv.inc(v.scalar_tensor_tensor(
                        h3b[ic % 2][:, tg * 512:(tg + 1) * 512], psu[i4 % 2][:],
                        sm["supc"][:, ic:ic + 1], gbuf[i4 % 2][:],
                        ALU.mult, ALU.mult)) == ev[f"h3{i4}"]
                ev[f"hm{ic}"] = gv.inc(v.reduce_max(
                    sm["h3mx"][:, ic:ic + 1], h3b[ic % 2][:],
                    axis=AX.X, apply_absolute_value=True))
                if ic % 3 == 2 or ic >= 20:
                    emit_down_v(v)

        @blk.scalar
        def _(s):
            for ic in range(NI):
                for tg in range(4):
                    i4 = ic * 4 + tg
                    s.wait_ge(tv.h, ev[f"gmm{i4}"])
                    if i4 >= 2:
                        s.wait_ge(vv.h, ev[f"h3{i4 - 2}"])
                    assert sv.inc(s.activation(
                        gbuf[i4 % 2][:], psg[i4 % 2][:], AF.Silu,
                        scale=sm["sgatec"][:, ic:ic + 1])) == ev[f"g{i4}"]
                s.wait_ge(vv.h, ev[f"h3{ic * 4 + 3}"])
                assert qh3s.inc(s.dma_start(
                    out=h3_dram[ic * 128:(ic + 1) * 128, :], in_=h3b[ic % 2][:])) \
                    == ev[f"h3s{ic}"]

        @blk.gpsimd
        def _(g):
            assert qdwl.inc(g.dma_start(
                out=wnd[:], in_=wdown[0:128, :])) == ev["dwl0"]
            for k in range(1, 8):
                g.wait_ge(dpg.h, ev[f"dwc{k - 1}"])
                assert qdwl.inc(g.dma_start(
                    out=wnd[:], in_=wdown[k * 128:(k + 1) * 128, :])) == ev[f"dwl{k}"]
                j = k - 1
                assert qdws.inc(g.dma_start(
                    out=wdq_dram[j * 128:(j + 1) * 128, :], in_=wqbd[j % 2][:])) \
                    == ev[f"dws{j}"]
            g.wait_ge(dpg.h, ev["dwc7"])
            assert qdws.inc(g.dma_start(
                out=wdq_dram[7 * 128:8 * 128, :], in_=wqbd[7 % 2][:])) == ev["dws7"]

    free(*[f"wgt{i}" for i in range(NCC)], *[f"wut{i}" for i in range(NCC)],
         *[f"xq_fm{i}" for i in range(NCC)], "gbuf0", "gbuf1", "wnd", "wqnd",
         "wqbd0", "wqbd1", "h3b0", "h3b1")

    if "h3" in taps:
        with nc.Block() as blk:
            @blk.sync
            def _(sync):
                wq_all(sync, qh3s)
                qtap.inc(sync.dma_start(out=tap_outs["h3"], in_=h3_dram))
                wq_all(sync, qtap)

    if stop_after == "p6":
        if wp_next[0] < WPREP or wp_sc_next[0] < WPREP:
            with nc.Block() as blk:
                @blk.vector
                def _(v):
                    emit_wprep_v(v, WPREP)

                @blk.scalar
                def _(s):
                    emit_wprep_sc(s, WPREP)
                    emit_wprep_sc_tail(s)
        es.close()
        return nc, tap_outs

    h3q = [alloc(f"h3q{i}", [128, N], BF16) for i in range(NI)]
    wdt = [alloc(f"wdt{i}", [128, C], BF16) for i in range(NI)]
    h3rl = [alloc(f"h3rl{i}", [128, N], F32) for i in range(2)]
    q32b = [alloc(f"q32b{i}", [128, N], I32) for i in range(2)]

    with nc.Block() as blk:
        @blk.vector
        def _(v):
            for ic in range(NI):
                v.wait_ge(gv.h, ev[f"hm{ic}"])
            ev["lmax2"] = vv.inc(v.reduce_max(sm["scal"][:, 3:4], sm["h3mx"][:], axis=AX.X))

        @blk.gpsimd
        def _(g):
            g.wait_ge(vv.h, ev["lmax2"])
            ev["cc2i"] = qcc2.inc(g.dma_start(
                out=cc2_in.ap().rearrange("a (b p) -> p (a b)", p=128),
                in_=sm["scal"][:, 3:4], allow_slow_non_contiguous=True))
            wq(g, ev["cc2i"])
            ev["cc2"] = ccs.inc(g.collective_compute(
                "AllReduce", ALU.max, replica_groups=[list(range(8))],
                ins=[cc2_in.ap().opt()], outs=[cc2_out.ap().opt()]))
            g.wait_ge(ccs.h, ev["cc2"])
            ev["mx2r"] = qcc2.inc(g.dma_start(out=sm["tinyrow"][:], in_=cc2_out.ap()))

        @blk.vector
        def _(v):
            wq(v, ev["mx2r"])
            ev["mx2"] = vv.inc(v.reduce_max(sm["tiny1"][:], sm["tinyrow"][:], axis=AX.X))

        @blk.sync
        def _(sync):
            sync.wait_ge(vv.h, ev["mx2"])
            ev["mx2d"] = qmx2.inc(sync.dma_start(
                out=vec_dram[1:2, 0:1], in_=sm["tiny1"][:]))
            wq(sync, ev["mx2d"])
            ev["gmax2"] = qmx2.inc(sync.dma_start(
                out=sm["scal"][:, 3:4],
                in_=vec_dram[1:2, 0:1].unsqueeze(1).broadcast_to([1, 128, 1])))

        @blk.scalar
        def _(s):
            wq(s, ev["gmax2"])
            ev["sigd"] = sv.inc(s.activation(sm["scal"][:, 7:8], sm["scal"][:, 9:10], AF.Sigmoid))

    for ic in range(NI):
        ev[f"h3r{ic}"] = qh3r.tok(ic)

    with nc.Block() as blk:
        @blk.vector
        def _(v):
            wq_all(v, qh3s)
            v.wait_ge(sv.h, ev["sigd"])
            evr = va.inc(v.tensor_tensor(sm["scal"][:, 11:12], sm["scal"][:, 3:4], sm["scal"][:, 7:8], ALU.mult))
            v.wait_ge(va.h, evr)
            evr = va.inc(v.tensor_scalar(sm["scal"][:, 4:5], sm["scal"][:, 11:12], sm["rq"][:, 0:1], None, ALU.mult))
            v.wait_ge(va.h, evr)
            ev["s2"] = va.inc(v.reciprocal(sm["scal"][:, 5:6], sm["scal"][:, 4:5]))
            v.wait_ge(va.h, ev["s2"])
            evr = va.inc(v.tensor_tensor(sm["scr2k"][:, 0:8], sm["sdc"][:], sm["gbc"][:], ALU.mult))
            v.wait_ge(va.h, evr)
            ev["coefc"] = vv.inc(v.tensor_scalar(
                sm["sdc"][:], sm["scr2k"][:, 0:8], sm["scal"][:, 4:5], None, ALU.mult))
            for ic in range(NI):
                wq(v, ev[f"h3r{ic}"])
                for hf_ in range(4):
                    sl_ = slice(hf_ * 512, (hf_ + 1) * 512)
                    v.tensor_scalar(sm["scr2k"][:, 0:512], h3rl[ic % 2][:, sl_],
                                    sm["scal"][:, 5:6], None, ALU.mult)
                    v.tensor_scalar(sm["scr2k"][:, 512:1024], sm["scr2k"][:, 0:512],
                                    sm["magic"][:, 0:1], None, ALU.add)
                    v.tensor_scalar(q32b[ic % 2][:, sl_], sm["scr2k"][:, 512:1024],
                                    sm["magic"][:, 0:1], None, ALU.subtract)
                ev[f"h3q{ic}"] = vv.inc(v.tensor_scalar(
                    h3q[ic][:], q32b[ic % 2][:],
                    QMAX, -QMAX - 1.0, ALU.min, ALU.max))

        @blk.scalar
        def _(s):
            s.wait_ge(va.h, ev["s2"])
            for ic in range(NI):
                if ic >= 2:
                    s.wait_ge(vv.h, ev[f"h3q{ic - 2}"])
                wq(s, ev[f"h3s{ic}"])
                assert qh3r.inc(s.dma_start(
                    out=h3rl[ic % 2][:], in_=h3_dram[ic * 128:(ic + 1) * 128, :])) \
                    == ev[f"h3r{ic}"]

        @blk.sync
        def _(sync):
            wq_all(sync, qdws, qh3s)
            for ic in range(NI):
                qwdt.inc(sync.dma_start(
                    out=wdt[ic][:],
                    in_=wdq_dram[:, ic * 128:(ic + 1) * 128], transpose=True))
            sync.wait_ge(vv.h, ev["coefc"])
            ev["coefd"] = qcoef.inc(sync.dma_start(
                out=row_dram[4:5, :].rearrange("a (b p) -> p (a b)", p=128),
                in_=sm["sdc"][:], allow_slow_non_contiguous=True))
            wq(sync, ev["coefd"])
            ev["coefr"] = qcoef.inc(sync.dma_start(
                out=sm["coef_rep"][:],
                in_=row_dram[4:5, :].unsqueeze(1).broadcast_to([1, 128, C])))

    if "s2" in taps:
        with nc.Block() as blk:
            @blk.sync
            def _(sync):
                qtap.inc(sync.dma_start(out=tap_outs["s2"], in_=sm["scal"][:, 4:5]))
                wq_all(sync, qtap)

    free("h3rl0", "h3rl1", "q32b0", "q32b1")

    if stop_after == "p7":
        if wp_next[0] < WPREP or wp_sc_next[0] < WPREP:
            with nc.Block() as blk:
                @blk.vector
                def _(v):
                    emit_wprep_v(v, WPREP)

                @blk.scalar
                def _(s):
                    emit_wprep_sc(s, WPREP)
                    emit_wprep_sc_tail(s)
        es.close()
        return nc, tap_outs

    x1rl = [alloc(f"x1rl{i}", [128, C], F32) for i in range(2)]
    x2c = [alloc(f"x2c{i}", [128, C], F32) for i in range(2)]
    psd = [bank(0, 2), bank(2, 2)]

    vv0 = vv.n
    for tc in range(NT):
        ev[f"xd{tc}"] = vv0 + 2 * tc + 1
        ev[f"x2{tc}"] = vv0 + 2 * tc + 2
    for tc in range(NT):
        ev[f"st{tc}"] = qso.tok(tc)
        ev[f"x1r{tc}"] = qx1r.tok(tc)

    with nc.Block() as blk:
        @blk.tensor
        def _(t):
            wq_all(t, qwdt)
            for ic in range(NI):
                t.wait_ge(vv.h, ev[f"h3q{ic}"])
            for tc in range(NT):
                if tc >= 2:
                    t.wait_ge(vv.h, ev[f"xd{tc - 2}"])
                for ic in range(NI):
                    for j in range(2):
                        mm = t.matmul(psd[tc % 2][:, j * 512:(j + 1) * 512],
                                      h3q[ic][:, tc * 128:(tc + 1) * 128],
                                      wdt[ic][:, j * 512:(j + 1) * 512],
                                      start=(ic == 0), stop=(ic == NI - 1),
                                      skip_group_check=True)
                ev[f"dn{tc}"] = tv.inc(mm)

        @blk.scalar
        def _(s):
            wq_all(s, qx1s)
            for tc in range(NT):
                if tc >= 2:
                    s.wait_ge(vv.h, ev[f"x2{tc - 2}"])
                assert qx1r.inc(s.dma_start(
                    out=x1rl[tc % 2][:],
                    in_=x1_dram.rearrange("(a p) c -> p a c", p=128)[:, tc, :])) \
                    == ev[f"x1r{tc}"]

        @blk.vector
        def _(v):
            wq(v, ev["coefr"])
            for tc in range(NT):
                v.wait_ge(tv.h, ev[f"dn{tc}"])
                if tc >= 2:
                    wq(v, ev[f"st{tc - 2}"])
                assert vv.inc(v.tensor_tensor(
                    x2c[tc % 2][:], psd[tc % 2][:], sm["coef_rep"][:], ALU.mult)) == ev[f"xd{tc}"]
                wq(v, ev[f"x1r{tc}"])
                assert vv.inc(v.tensor_tensor(
                    x2c[tc % 2][:], x2c[tc % 2][:], x1rl[tc % 2][:], ALU.add)) == ev[f"x2{tc}"]

        @blk.sync
        def _(sync):
            for tc in range(NT):
                sync.wait_ge(vv.h, ev[f"x2{tc}"])
                assert qso.inc(sync.dma_start(
                    out=y_out.rearrange("(a p) c -> p a c", p=128)[:, tc, :],
                    in_=x2c[tc % 2][:])) == ev[f"st{tc}"]
            wq_all(sync, qso)

    es.close()
    return nc, tap_outs


def kernel(**inputs):
    nc, _ = build_nc()
    in_maps = []
    for b in range(8):
        in_maps.append({
            "x": np.ascontiguousarray(inputs["L0_attn_x"][b]),
            "w_qkv": inputs["L0_attn_w_qkv"],
            "w_proj": inputs["L0_attn_w_proj"],
            "b_proj": inputs["L0_attn_b_proj"],
            "w_dwc": inputs["L0_attn_w_dwc"],
            "b_dwc": inputs["L0_attn_b_dwc"],
            "clip_x": inputs["L0_ffn_clip_x"],
            "clip_down": inputs["L0_ffn_clip_down"],
            "w_gate": inputs["L0_ffn_w_gate"],
            "w_up": inputs["L0_ffn_w_up"],
            "w_down": inputs["L0_ffn_w_down"],
            "c_row": np.ascontiguousarray(inputs["L0_c"][b]),
            "wmod": inputs["L0_wmod"],
        })
    res = run_bass_kernel_spmd(nc, in_maps, core_ids=list(range(8)))
    return np.stack([r["y"] for r in res.results], axis=0)


if __name__ == "__main__":
    nc, _ = build_nc()
    print("built ok:", nc.n_instructions(), "instructions")
